# version 03
# baseline (speedup 1.0000x reference)
import numpy as np
import ml_dtypes
from contextlib import ExitStack

import concourse.bass as bass
import concourse.mybir as mybir
from concourse.bass import IndirectOffsetOnAxis
from concourse.bass_utils import run_bass_kernel_spmd

F32 = mybir.dt.float32
BF16 = mybir.dt.bfloat16
I32 = mybir.dt.int32
U32 = mybir.dt.uint32
AF = mybir.ActivationFunctionType
ALU = mybir.AluOpType
AX = mybir.AxisListType

NCORES = 8
L = 4096
T = 1024
TT = T // 128
HT = L // 128
NF = 2049
FP = 2176
FT = FP // 128
IF = 14336
IT = IF // 128
RG = IF // NCORES
RGT = RG // 128
RD = L // NCORES
RDT = RD // 128
CB = 2048
EPS = 1e-5

AF_WBUF = (0, 2048)
AF_SVS = (4096, 6144)
AF_SUT = 8192
AF_XBUF = (0, 4096)
AF_XNBUF = 8192
AF_SQ = (0, 512, 1024, 1536)
AF_ROLL = (0, 4096)
AF_ACC = (8192, 12288)
AF_SA = (0, 1024)
AF_DNB = 0
AF_XCK = (4096, 4608)
AF_ACK = (5120, 5632)
AF_OB = (6144, 6656)

B1_XNT = 0
B1_P = 32768
B1_WOUT = (0, 2048)
B1_WSL = (32768, 40960)

B2_CS = (0, 8192)
B2_DSL = (0, 8704)
B2_WSL = (0, 8192)
B2_SA = (16384, 17408)
B2_DB = (17408, 18432)
B2_WD = (0, 512)
B2_DLD = (1024, 2048)


def _build_tables():
    k = np.arange(L)
    cosv = np.cos(2 * np.pi * k / L)
    sinv = np.sin(2 * np.pi * k / L)
    h = np.arange(L)[:, None]
    f = np.arange(NF)[None, :]
    m = (h * f) % L
    ctab = np.zeros((L, 2 * FP), dtype=np.float32)
    ctab[:, :NF] = cosv[m]
    ctab[:, FP:FP + NF] = sinv[m]
    ctab_bf = ctab.astype(ml_dtypes.bfloat16)

    fr = np.arange(NF)[:, None]
    d = np.arange(L)[None, :]
    m2 = (fr * d) % L
    coef = np.full((NF, 1), 2.0 / L)
    coef[0, 0] = 1.0 / L
    coef[NF - 1, 0] = 1.0 / L
    dtab = np.zeros((FP, L), dtype=np.float32)
    dtab[:NF, :] = coef * cosv[m2]
    dtab_bf = dtab.astype(ml_dtypes.bfloat16)

    ident = np.eye(128, dtype=np.float32)
    p = np.arange(128, dtype=np.int64)[:, None]
    j = np.arange(TT, dtype=np.int64)[None, :]
    rb = ((j * 128 + p) * (2 * L)).astype(np.int32)
    rball = np.repeat(rb, 8, axis=1)
    return ctab_bf, dtab_bf, ident, rball


def build_nc():
    nc = bass.Bass("TRN2", target_bir_lowering=False)

    x_in = nc.dram_tensor("x", [T, L], F32, kind="ExternalInput").ap()
    gw = nc.dram_tensor("gw", [IF, L], F32, kind="ExternalInput").ap()
    gsu = nc.dram_tensor("gsu", [IF, 4], F32, kind="ExternalInput").ap()
    gsv = nc.dram_tensor("gsv", [4, L], F32, kind="ExternalInput").ap()
    uw = nc.dram_tensor("uw", [IF, L], F32, kind="ExternalInput").ap()
    usu = nc.dram_tensor("usu", [IF, 4], F32, kind="ExternalInput").ap()
    usv = nc.dram_tensor("usv", [4, L], F32, kind="ExternalInput").ap()
    dw = nc.dram_tensor("dw", [L, IF], F32, kind="ExternalInput").ap()
    dsu = nc.dram_tensor("dsu", [L, 4], F32, kind="ExternalInput").ap()
    dsv = nc.dram_tensor("dsv", [4, IF], F32, kind="ExternalInput").ap()
    ctab = nc.dram_tensor("ctab", [L, 2 * FP], BF16, kind="ExternalInput").ap()
    dtab = nc.dram_tensor("dtab", [FP, L], BF16, kind="ExternalInput").ap()
    identd = nc.dram_tensor("identd", [128, 128], F32, kind="ExternalInput").ap()
    rbd = nc.dram_tensor("rbd", [128, 8 * TT], I32, kind="ExternalInput").ap()
    y_out = nc.dram_tensor("y", [T, L], F32, kind="ExternalOutput").ap()

    xn2 = nc.dram_tensor("xn2", [T, 2 * L], F32, kind="Internal").ap()
    attn = nc.dram_tensor("attn", [T, L], F32, kind="Internal").ap()
    xnT_sp = nc.dram_tensor("xnT_sp", [L, T], BF16, kind="Internal").ap()
    dscr = nc.dram_tensor("dscr", [IF, T], BF16, kind="Internal").ap()
    wgl = nc.dram_tensor("wgl", [IF, L], BF16, kind="Internal")
    wul = nc.dram_tensor("wul", [IF, L], BF16, kind="Internal")
    wdl = nc.dram_tensor("wdl", [L, IF], BF16, kind="Internal")

    ctx = ExitStack()
    with ctx:
        ident_s = ctx.enter_context(nc.sbuf_tensor("ident_s", [128, 128], F32))
        rb_s = ctx.enter_context(nc.sbuf_tensor("rb_s", [128, 64], I32))
        sus = ctx.enter_context(nc.sbuf_tensor("sus", [128, 1024], F32))
        suTjit = ctx.enter_context(nc.sbuf_tensor("suTjit", [4, 256], F32))
        dly_u = ctx.enter_context(nc.sbuf_tensor("dly_u", [128, 64], U32))
        dly_i = ctx.enter_context(nc.sbuf_tensor("dly_i", [128, 64], I32))
        gidx = ctx.enter_context(nc.sbuf_tensor("gidx", [128, 64], I32))
        w8 = ctx.enter_context(nc.sbuf_tensor("w8", [128, 64], F32))
        stat = ctx.enter_context(nc.sbuf_tensor("stat", [128, 64], F32))
        v8b = ctx.enter_context(nc.sbuf_tensor("v8b", [128, 8], BF16))
        arF = ctx.enter_context(nc.sbuf_tensor("arF", [128, 16384], F32))
        arB1 = ctx.enter_context(nc.sbuf_tensor("arB1", [128, 50176], BF16))
        arB2 = ctx.enter_context(nc.sbuf_tensor("arB2", [128, 19456], BF16))
        pgA = ctx.enter_context(nc.psum_tensor("pgA", [128, 2048], F32))
        pgB = ctx.enter_context(nc.psum_tensor("pgB", [128, 2048], F32))
        pg = (pgA, pgB)

        qS = [ctx.enter_context(nc.semaphore("qS")), 0]
        qA = [ctx.enter_context(nc.semaphore("qA")), 0]
        qG = [ctx.enter_context(nc.semaphore("qG")), 0]
        qY = [ctx.enter_context(nc.semaphore("qY")), 0]
        sV = [ctx.enter_context(nc.semaphore("sV")), 0]
        sP = [ctx.enter_context(nc.semaphore("sP")), 0]
        sC = [ctx.enter_context(nc.semaphore("sC")), 0]

        block = ctx.enter_context(nc.Block())

        GUT = IF // 128
        DTT = L // 128
        mats = [
            ("g", gw, gsu, gsv, GUT, L, wgl),
            ("u", uw, usu, usv, GUT, L, wul),
            ("d", dw, dsu, dsv, DTT, IF, wdl),
        ]
        suoff = [0, 4 * GUT, 8 * GUT]

        plan0 = []
        units = []
        ci = 0
        for mi, (mn, w_, su_, sv_, RT, C, loc) in enumerate(mats):
            for rt in range(RT):
                u = len(units)
                first = ci
                for cb in range(C // CB):
                    plan0.append((mi, rt, cb, u, first))
                    ci += 1
                units.append((mi, rt, first, ci - 1))
        n0 = len(plan0)
        nu = len(units)
        NSC = CB // 512

        qS0 = qS[1]
        qs_chunk = [qS0 + 80 + 32 * (i + 1) for i in range(n0)]
        sP0 = sP[1]
        pe_tr = [0] * nu
        pe_g = [0] * n0
        c_ = sP0
        for u, (mi, rt, first, lastc) in enumerate(units):
            c_ += 1
            pe_tr[u] = c_
            for i in range(first, lastc + 1):
                c_ += NSC
                pe_g[i] = c_
        sV0 = sV[1]
        v_tr = [0] * nu
        v_mul = [0] * n0
        c_ = sV0
        for u, (mi, rt, first, lastc) in enumerate(units):
            c_ += 1
            v_tr[u] = c_
            for i in range(first, lastc + 1):
                c_ += 1
                v_mul[i] = c_
        qA0 = qA[1]
        a_st = [qA0 + 16 * (i + 1) for i in range(n0)]

        @block.sync
        def _(q):
            q.dma_start(out=ident_s[:], in_=identd).then_inc(qS[0], 16)
            q.dma_start(out=rb_s[:], in_=rbd).then_inc(qS[0], 16)
            for mi, (mn, w_, su_, sv_, RT, C, loc) in enumerate(mats):
                q.dma_start(out=sus[:, suoff[mi]:suoff[mi] + 4 * RT]
                            .rearrange("p (r c) -> p r c", c=4),
                            in_=su_.rearrange("(r p) c -> p r c", p=128)
                            ).then_inc(qS[0], 16)
            qS[1] += 80
            for i, (mi, rt, cb, u, first) in enumerate(plan0):
                mn, w_, su_, sv_, RT, C, loc = mats[mi]
                if i >= 2:
                    q.wait_ge(sV[0], v_mul[i - 2])
                sl = AF_SVS[i % 2]
                q.dma_start(out=arF[0:4, sl:sl + CB],
                            in_=sv_[:, cb * CB:(cb + 1) * CB]).then_inc(qS[0], 16)
                wl = AF_WBUF[i % 2]
                q.dma_start(out=arF[:, wl:wl + CB],
                            in_=w_[rt * 128:(rt + 1) * 128, cb * CB:(cb + 1) * CB]
                            ).then_inc(qS[0], 16)
                qS[1] += 32
            assert qS[1] == qs_chunk[-1]

        @block.tensor
        def _(t):
            t.wait_ge(qS[0], qS0 + 80)
            for u, (mi, rt, first, lastc) in enumerate(units):
                if first >= 2:
                    t.wait_ge(sV[0], v_mul[first - 2])
                if u >= 2:
                    pass
                t.transpose(out=pg[first % 2][0:4, 0:128],
                            in_=sus[:, suoff[mi] + rt * 4: suoff[mi] + rt * 4 + 4],
                            identity=ident_s[:]).then_inc(sP[0], 1)
                sP[1] += 1
                assert sP[1] == pe_tr[u]
                for i in range(first, lastc + 1):
                    t.wait_ge(qS[0], qs_chunk[i])
                    t.wait_ge(sV[0], v_tr[u])
                    if i >= 2:
                        t.wait_ge(sV[0], v_mul[i - 2])
                    sl = AF_SVS[i % 2]
                    for sc_ in range(NSC):
                        ins = t.matmul(out=pg[i % 2][:, sc_ * 512:(sc_ + 1) * 512],
                                       lhsT=suTjit[0:4, (u % 2) * 128:(u % 2) * 128 + 128],
                                       rhs=arF[0:4, sl + sc_ * 512: sl + (sc_ + 1) * 512],
                                       start=True, stop=True)
                        ins.then_inc(sP[0], 1)
                        sP[1] += 1
                    assert sP[1] == pe_g[i]

        @block.vector
        def _(v):
            for u, (mi, rt, first, lastc) in enumerate(units):
                v.wait_ge(sP[0], pe_tr[u])
                if u >= 2:
                    v.wait_ge(sP[0], pe_g[units[u - 2][3]])
                v.tensor_copy(out=suTjit[0:4, (u % 2) * 128:(u % 2) * 128 + 128],
                              in_=pg[first % 2][0:4, 0:128]).then_inc(sV[0], 1)
                sV[1] += 1
                assert sV[1] == v_tr[u]
                for i in range(first, lastc + 1):
                    v.wait_ge(sP[0], pe_g[i])
                    v.wait_ge(qS[0], qs_chunk[i])
                    if i >= 2:
                        v.wait_ge(qA[0], a_st[i - 2])
                    wl = AF_WBUF[i % 2]
                    bo = B1_WOUT[i % 2]
                    for sc_ in range(NSC):
                        ins = v.tensor_mul(out=arB1[:, bo + sc_ * 512: bo + (sc_ + 1) * 512],
                                           in0=arF[:, wl + sc_ * 512: wl + (sc_ + 1) * 512],
                                           in1=pg[i % 2][:, sc_ * 512:(sc_ + 1) * 512])
                    ins.then_inc(sV[0], 1)
                    sV[1] += 1
                    assert sV[1] == v_mul[i]

        @block.scalar
        def _(s):
            for i, (mi, rt, cb, u, first) in enumerate(plan0):
                mn, w_, su_, sv_, RT, C, loc = mats[mi]
                s.wait_ge(sV[0], v_mul[i])
                bo = B1_WOUT[i % 2]
                s.dma_start(out=loc.ap()[rt * 128:(rt + 1) * 128, cb * CB:(cb + 1) * CB],
                            in_=arB1[:, bo:bo + CB]).then_inc(qA[0], 16)
                qA[1] += 16
            assert qA[1] == a_st[-1]

        ph0_pe_end = sP[1]
        ph0_v_end = sV[1]
        ph0_gu_stored = a_st[447]
        ph0_all_stored = a_st[-1]

        qS1 = qS[1]
        q_x = [qS1 + 16 * (j + 1) for j in range(TT)]
        sV1 = sV[1]
        v_stat = [sV1 + 34 * j + 1 for j in range(TT)]
        v_apply = [sV1 + 34 * j + 2 for j in range(TT)]
        v_evac = [[sV1 + 34 * j + 2 + (jh + 1) for jh in range(HT)] for j in range(TT)]
        sC1 = sC[1]
        c_sqrt = [sC1 + j + 1 for j in range(TT)]
        sP1 = sP[1]
        p_xp = [[sP1 + HT * j + jh + 1 for jh in range(HT)] for j in range(TT)]
        qA1 = qA[1]
        a_xn2 = [qA1 + 32 * (j + 1) for j in range(TT)]

        @block.sync
        def _(q):
            for j in range(TT):
                if j == 0:
                    q.wait_ge(sP[0], ph0_pe_end)
                    q.wait_ge(sV[0], ph0_v_end)
                if j >= 2:
                    q.wait_ge(sV[0], v_apply[j - 2])
                xo = AF_XBUF[j % 2]
                q.dma_start(out=arF[:, xo:xo + L],
                            in_=x_in[j * 128:(j + 1) * 128, :]).then_inc(qS[0], 16)
                qS[1] += 16

        @block.vector
        def _(v):
            v.wait_ge(qA[0], ph0_all_stored)
            for j in range(TT):
                xo = AF_XBUF[j % 2]
                v.wait_ge(qS[0], q_x[j])
                for c in range(8):
                    v.bn_stats(out=stat[:, 6 * c: 6 * (c + 1)],
                               in_=arF[:, xo + 512 * c: xo + 512 * (c + 1)])
                v.drain()
                v.bn_aggr(out=stat[:, 48:50], in_=stat[:, 0:48])
                v.drain()
                v.tensor_scalar_add(stat[:, 50:51], stat[:, 49:50], EPS)
                v.drain()
                v.sem_inc(sV[0], 1)
                sV[1] += 1
                v.wait_ge(sC[0], c_sqrt[j])
                v.reciprocal(out=stat[:, 52:53], in_=stat[:, 51:52])
                v.drain()
                if j >= 1:
                    v.wait_ge(sP[0], p_xp[j - 1][-1])
                    v.wait_ge(qA[0], a_xn2[j - 1])
                v.tensor_scalar(out=arF[:, AF_XNBUF:AF_XNBUF + L],
                                in0=arF[:, xo:xo + L],
                                scalar1=stat[:, 48:49], scalar2=stat[:, 52:53],
                                op0=ALU.subtract, op1=ALU.mult)
                v.drain()
                v.sem_inc(sV[0], 1)
                sV[1] += 1
                assert sV[1] == v_apply[j]
                for jh in range(HT):
                    v.wait_ge(sP[0], p_xp[j][jh])
                    dst = B1_XNT + jh * T + j * 128
                    v.tensor_copy(out=arB1[:, dst:dst + 128],
                                  in_=pg[jh % 2][:, 0:128]).then_inc(sV[0], 1)
                    sV[1] += 1
                assert sV[1] == v_evac[j][-1]

        @block.scalar
        def _(s):
            for j in range(TT):
                s.wait_ge(sV[0], v_stat[j])
                s.activation(out=stat[:, 51:52], in_=stat[:, 50:51],
                             func=AF.Sqrt).then_inc(sC[0], 1)
                sC[1] += 1
                s.wait_ge(sV[0], v_apply[j])
                s.dma_start(out=xn2[j * 128:(j + 1) * 128, 0:L],
                            in_=arF[:, AF_XNBUF:AF_XNBUF + L]).then_inc(qA[0], 16)
                s.dma_start(out=xn2[j * 128:(j + 1) * 128, L:2 * L],
                            in_=arF[:, AF_XNBUF:AF_XNBUF + L]).then_inc(qA[0], 16)
                qA[1] += 32
            assert qA[1] == a_xn2[-1] and sC[1] == c_sqrt[-1]

        @block.tensor
        def _(t):
            for j in range(TT):
                t.wait_ge(sV[0], v_apply[j])
                for jh in range(HT):
                    if jh >= 2:
                        t.wait_ge(sV[0], v_evac[j][jh - 2])
                    elif j >= 1:
                        t.wait_ge(sV[0], v_evac[j - 1][HT - 2 + jh])
                    t.transpose(out=pg[jh % 2][:, 0:128],
                                in_=arF[:, AF_XNBUF + jh * 128: AF_XNBUF + (jh + 1) * 128],
                                identity=ident_s[:]).then_inc(sP[0], 1)
                    sP[1] += 1
                assert sP[1] == p_xp[j][-1]

        ph1_v_end = sV[1]
        ph1_p_end = sP[1]

        qS2 = qS[1]
        q_cs = [qS2 + 16 * 64 * (jf + 1) for jf in range(FT)]
        sP2 = sP[1]
        pe_jf = [sP2 + 4 * (jf + 1) for jf in range(FT)]
        sC2 = sC[1]
        c_sq = [sC2 + (jf + 1) for jf in range(FT)]
        sV2 = sV[1]
        v_P = [sV2 + (jf + 1) for jf in range(FT)]

        @block.sync
        def _(q):
            for jf in range(FT):
                if jf >= 2:
                    q.wait_ge(sP[0], pe_jf[jf - 2])
                cso = B2_CS[jf % 2]
                for jh in range(HT):
                    q.dma_start(out=arB2[:, cso + jh * 128: cso + (jh + 1) * 128],
                                in_=ctab[jh * 128:(jh + 1) * 128, jf * 128:(jf + 1) * 128]
                                ).then_inc(qS[0], 16)
                    q.dma_start(out=arB2[:, cso + 4096 + jh * 128: cso + 4096 + (jh + 1) * 128],
                                in_=ctab[jh * 128:(jh + 1) * 128, FP + jf * 128: FP + (jf + 1) * 128]
                                ).then_inc(qS[0], 16)
                qS[1] += 16 * 64
            assert qS[1] == q_cs[-1]

        @block.tensor
        def _(t):
            t.wait_ge(sV[0], ph1_v_end)
            for jf in range(FT):
                t.wait_ge(qS[0], q_cs[jf])
                if jf >= 2:
                    t.wait_ge(sC[0], c_sq[jf - 2])
                cso = B2_CS[jf % 2]
                ps = pg[jf % 2]
                for jh in range(HT):
                    st_ = (jh == 0)
                    sp_ = (jh == HT - 1)
                    for tg in range(2):
                        ins = t.matmul(out=ps[:, tg * 512:(tg + 1) * 512],
                                       lhsT=arB2[:, cso + jh * 128: cso + (jh + 1) * 128],
                                       rhs=arB1[:, B1_XNT + jh * T + tg * 512:
                                                B1_XNT + jh * T + (tg + 1) * 512],
                                       start=st_, stop=sp_)
                        if sp_:
                            ins.then_inc(sP[0], 1)
                            sP[1] += 1
                    for tg in range(2):
                        ins = t.matmul(out=ps[:, 1024 + tg * 512: 1024 + (tg + 1) * 512],
                                       lhsT=arB2[:, cso + 4096 + jh * 128: cso + 4096 + (jh + 1) * 128],
                                       rhs=arB1[:, B1_XNT + jh * T + tg * 512:
                                                B1_XNT + jh * T + (tg + 1) * 512],
                                       start=st_, stop=sp_)
                        if sp_:
                            ins.then_inc(sP[0], 1)
                            sP[1] += 1
                assert sP[1] == pe_jf[jf]

        @block.scalar
        def _(s):
            for jf in range(FT):
                s.wait_ge(sP[0], pe_jf[jf])
                if jf >= 1:
                    s.wait_ge(sV[0], v_P[jf - 1])
                ps = pg[jf % 2]
                for q_ in range(4):
                    ins = s.activation(out=arF[:, AF_SQ[q_]:AF_SQ[q_] + 512],
                                       in_=ps[:, q_ * 512:(q_ + 1) * 512],
                                       func=AF.Square)
                ins.then_inc(sC[0], 1)
                sC[1] += 1
            assert sC[1] == c_sq[-1]

        @block.vector
        def _(v):
            for jf in range(FT):
                v.wait_ge(sC[0], c_sq[jf])
                po = B1_P + jf * T
                v.tensor_add(out=arB1[:, po:po + 512],
                             in0=arF[:, AF_SQ[0]:AF_SQ[0] + 512],
                             in1=arF[:, AF_SQ[2]:AF_SQ[2] + 512])
                ins = v.tensor_add(out=arB1[:, po + 512:po + 1024],
                                   in0=arF[:, AF_SQ[1]:AF_SQ[1] + 512],
                                   in1=arF[:, AF_SQ[3]:AF_SQ[3] + 512])
                ins.then_inc(sV[0], 1)
                sV[1] += 1
            assert sV[1] == v_P[-1]

        @block.scalar
        def _(s):
            s.wait_ge(sP[0], pe_jf[-1])
            s.dma_start(out=xnT_sp.rearrange("(jh p) t -> p jh t", p=128),
                        in_=arB1[:, B1_XNT:B1_XNT + HT * T]
                        .rearrange("p (jh t) -> p jh t", t=T)).then_inc(qA[0], 16)
            qA[1] += 16

        a_spill = qA[1]
        ph2a_pe_end = sP[1]

        qS3 = qS[1]
        q_D = [qS3 + 16 * FT * (dc + 1) for dc in range(8)]
        sP3 = sP[1]
        pe_corr = [[sP3 + 8 * dc + jt + 1 for jt in range(TT)] for dc in range(8)]
        sV3 = sV[1]
        v_corr = [[sV3 + 8 * dc + jt + 1 for jt in range(TT)] for dc in range(8)]

        @block.sync
        def _(q):
            q.wait_ge(sP[0], ph2a_pe_end)
            for dc in range(8):
                if dc >= 2:
                    q.wait_ge(sP[0], pe_corr[dc - 2][-1])
                do = B2_DSL[dc % 2]
                for jfr in range(FT):
                    q.dma_start(out=arB2[:, do + jfr * 512: do + (jfr + 1) * 512],
                                in_=dtab[jfr * 128:(jfr + 1) * 128, dc * 512:(dc + 1) * 512]
                                ).then_inc(qS[0], 16)
                qS[1] += 16 * FT
            assert qS[1] == q_D[-1]

        @block.tensor
        def _(t):
            t.wait_ge(sC[0], c_sq[-1])
            for dc in range(8):
                t.wait_ge(qS[0], q_D[dc])
                do = B2_DSL[dc % 2]
                for jt in range(TT):
                    idx = 8 * dc + jt
                    if idx >= 2:
                        t.wait_ge(sV[0], v_corr[(idx - 2) // 8][(idx - 2) % 8])
                    ps = pg[jt % 2]
                    for jfr in range(FT):
                        ins = t.matmul(out=ps[:, 0:512],
                                       lhsT=arB1[:, B1_P + jfr * T + jt * 128:
                                                B1_P + jfr * T + (jt + 1) * 128],
                                       rhs=arB2[:, do + jfr * 512: do + (jfr + 1) * 512],
                                       start=(jfr == 0), stop=(jfr == FT - 1))
                    ins.then_inc(sP[0], 1)
                    sP[1] += 1
                assert sP[1] == pe_corr[dc][-1]

        @block.vector
        def _(v):
            v.wait_ge(qA[0], a_spill)
            for dc in range(8):
                for jt in range(TT):
                    v.wait_ge(sP[0], pe_corr[dc][jt])
                    dst = B1_XNT + jt * L + dc * 512
                    v.tensor_copy(out=arB1[:, dst:dst + 512],
                                  in_=pg[jt % 2][:, 0:512]).then_inc(sV[0], 1)
                    sV[1] += 1
            assert sV[1] == v_corr[-1][-1]

        ph2b_pe_end = sP[1]

        sV4 = sV[1]
        v_top = [sV4 + 2 * (jt + 1) for jt in range(TT)]
        sC4 = sC[1]
        c_exp = [sC4 + jt + 1 for jt in range(TT)]

        @block.vector
        def _(v):
            for jt in range(TT):
                cr = arB1[:, B1_XNT + jt * L: B1_XNT + (jt + 1) * L]
                v.drain()
                v.max(out=v8b[:], in_=cr)
                v.drain()
                v.max_index(out=dly_u[:, jt * 8:(jt + 1) * 8], in_max=v8b[:], in_values=cr)
                v.tensor_copy(out=stat[:, 0:8], in_=v8b[:])
                v.drain()
                v.tensor_scalar(out=stat[:, 8:16], in0=stat[:, 0:8],
                                scalar1=stat[:, 0:1], scalar2=None, op0=ALU.subtract)
                v.drain()
                v.sem_inc(sV[0], 1)
                sV[1] += 1
                v.wait_ge(sC[0], c_exp[jt])
                v.reduce_sum(out=stat[:, 24:25], in_=stat[:, 16:24], axis=AX.X)
                v.drain()
                v.reciprocal(out=stat[:, 25:26], in_=stat[:, 24:25])
                v.drain()
                v.tensor_scalar(out=w8[:, jt * 8:(jt + 1) * 8], in0=stat[:, 16:24],
                                scalar1=stat[:, 25:26], scalar2=None, op0=ALU.mult)
                v.tensor_copy(out=dly_i[:, jt * 8:(jt + 1) * 8],
                              in_=dly_u[:, jt * 8:(jt + 1) * 8])
                v.drain()
                v.tensor_tensor(out=gidx[:, jt * 8:(jt + 1) * 8],
                                in0=dly_i[:, jt * 8:(jt + 1) * 8],
                                in1=rb_s[:, jt * 8:(jt + 1) * 8],
                                op=ALU.add)
                v.drain()
                v.sem_inc(sV[0], 1)
                sV[1] += 1
            assert sV[1] == v_top[-1]

        @block.scalar
        def _(s):
            for jt in range(TT):
                s.wait_ge(sV[0], v_top[jt] - 1)
                s.activation(out=stat[:, 16:24], in_=stat[:, 8:16],
                             func=AF.Exp).then_inc(sC[0], 1)
                sC[1] += 1
            assert sC[1] == c_exp[-1]

        qG3 = qG[1]
        g_roll = [qG3 + 16 * (i + 1) for i in range(TT * 8)]
        sV5 = sV[1]
        v_acc = [sV5 + (i + 1) for i in range(TT * 8)]
        qA3 = qA[1]
        a_attn = [qA3 + 16 * (jt + 1) for jt in range(TT)]

        @block.gpsimd
        def _(g):
            g.wait_ge(qA[0], a_xn2[-1])
            for jt in range(TT):
                g.wait_ge(sV[0], v_top[jt])
                for k in range(8):
                    i = jt * 8 + k
                    if i >= 2:
                        g.wait_ge(sV[0], v_acc[i - 2])
                    ro = AF_ROLL[i % 2]
                    g.indirect_dma_start(
                        out=arF[:, ro:ro + L],
                        out_offset=None,
                        in_=xn2,
                        in_offset=IndirectOffsetOnAxis(gidx[:, i:i + 1], axis=1),
                    ).then_inc(qG[0], 16)
                    qG[1] += 16
            assert qG[1] == g_roll[-1]

        @block.vector
        def _(v):
            for jt in range(TT):
                for k in range(8):
                    i = jt * 8 + k
                    v.wait_ge(qG[0], g_roll[i])
                    ro = AF_ROLL[i % 2]
                    if k == 0:
                        v.tensor_scalar(out=arF[:, AF_ACC[0]:AF_ACC[0] + L],
                                        in0=arF[:, ro:ro + L],
                                        scalar1=w8[:, i:i + 1], scalar2=None,
                                        op0=ALU.mult)
                    else:
                        if k == 1 and jt >= 1:
                            v.wait_ge(qA[0], a_attn[jt - 1])
                        dsto = AF_ACC[k % 2]
                        srco = AF_ACC[(k + 1) % 2]
                        v.scalar_tensor_tensor(out=arF[:, dsto:dsto + L],
                                               in0=arF[:, ro:ro + L],
                                               scalar=w8[:, i:i + 1],
                                               in1=arF[:, srco:srco + L],
                                               op0=ALU.mult, op1=ALU.add)
                    v.drain()
                    v.sem_inc(sV[0], 1)
                    sV[1] += 1
            assert sV[1] == v_acc[-1]

        @block.scalar
        def _(s):
            for jt in range(TT):
                s.wait_ge(sV[0], v_acc[jt * 8 + 7])
                s.dma_start(out=attn[jt * 128:(jt + 1) * 128, :],
                            in_=arF[:, AF_ACC[1]:AF_ACC[1] + L]).then_inc(qA[0], 16)
                qA[1] += 16
            assert qA[1] == a_attn[-1]

        qA4 = qA[1]

        @block.scalar
        def _(s):
            s.wait_ge(sV[0], v_top[-1])
            s.dma_start(out=arB1[:, B1_XNT:B1_XNT + HT * T]
                        .rearrange("p (jh t) -> p jh t", t=T),
                        in_=xnT_sp.rearrange("(jh p) t -> p jh t", p=128)
                        ).then_inc(qA[0], 16)
            qA[1] += 16

        a_xnT2 = qA[1]

        NOI = IT
        qS4 = qS[1]
        q_w4 = [qS4 + 16 * 64 * (og + 1) for og in range(NOI // 2)]
        sP4 = sP[1]
        pe_gu = [sP4 + 4 * (oi + 1) for oi in range(NOI)]
        sC5 = sC[1]
        c_silu = [sC5 + (oi + 1) for oi in range(NOI)]
        sV6 = sV[1]
        v_d = [sV6 + (oi + 1) for oi in range(NOI)]
        qG4 = qG[1]
        g_d = [qG4 + 16 * (oi + 1) for oi in range(NOI)]

        def wslab(mat, side):
            if mat == 0:
                o = B2_WSL[side]
                return arB2[:, o:o + 8192]
            o = B1_WSL[side]
            return arB1[:, o:o + 8192]

        @block.sync
        def _(q):
            q.wait_ge(qA[0], ph0_gu_stored)
            q.wait_ge(sP[0], ph2b_pe_end)
            for og in range(NOI // 2):
                if og >= 2:
                    q.wait_ge(sP[0], pe_gu[2 * (og - 2) + 1])
                for mat, full in ((0, wgl), (1, wul)):
                    sl = wslab(mat, og % 2)
                    for jh in range(HT):
                        q.dma_start(out=sl[:, jh * 256:(jh + 1) * 256],
                                    in_=full.ap()[og * 256:(og + 1) * 256,
                                                  jh * 128:(jh + 1) * 128],
                                    transpose=True).then_inc(qS[0], 16)
                qS[1] += 16 * 64
            assert qS[1] == q_w4[-1]

        @block.tensor
        def _(t):
            t.wait_ge(qA[0], a_xnT2)
            for oi in range(NOI):
                og = oi // 2
                t.wait_ge(qS[0], q_w4[og])
                if oi >= 2:
                    t.wait_ge(sC[0], c_silu[oi - 2])
                    t.wait_ge(sV[0], v_d[oi - 2])
                ps = pg[oi % 2]
                for mat in range(2):
                    sl = wslab(mat, og % 2)
                    base = 0 if mat == 0 else 1024
                    for jh in range(HT):
                        st_ = (jh == 0)
                        sp_ = (jh == HT - 1)
                        lh = sl[:, jh * 256 + (oi % 2) * 128: jh * 256 + (oi % 2) * 128 + 128]
                        for tg in range(2):
                            ins = t.matmul(out=ps[:, base + tg * 512: base + (tg + 1) * 512],
                                           lhsT=lh,
                                           rhs=arB1[:, B1_XNT + jh * T + tg * 512:
                                                    B1_XNT + jh * T + (tg + 1) * 512],
                                           start=st_, stop=sp_)
                            if sp_:
                                ins.then_inc(sP[0], 1)
                                sP[1] += 1
                assert sP[1] == pe_gu[oi]

        @block.scalar
        def _(s):
            for oi in range(NOI):
                s.wait_ge(sP[0], pe_gu[oi])
                if oi >= 2:
                    s.wait_ge(sV[0], v_d[oi - 2])
                ps = pg[oi % 2]
                sao = AF_SA[oi % 2]
                for tg in range(2):
                    ins = s.activation(out=arF[:, sao + tg * 512: sao + (tg + 1) * 512],
                                       in_=ps[:, tg * 512:(tg + 1) * 512],
                                       func=AF.Silu)
                ins.then_inc(sC[0], 1)
                sC[1] += 1
            assert sC[1] == c_silu[-1]

        @block.vector
        def _(v):
            for oi in range(NOI):
                v.wait_ge(sC[0], c_silu[oi])
                if oi >= 2:
                    v.wait_ge(qG[0], g_d[oi - 2])
                ps = pg[oi % 2]
                sao = AF_SA[oi % 2]
                dbo = B2_DB[oi % 2]
                for tg in range(2):
                    ins = v.tensor_mul(out=arB2[:, dbo + tg * 512: dbo + (tg + 1) * 512],
                                       in0=arF[:, sao + tg * 512: sao + (tg + 1) * 512],
                                       in1=ps[:, 1024 + tg * 512: 1024 + (tg + 1) * 512])
                ins.then_inc(sV[0], 1)
                sV[1] += 1
            assert sV[1] == v_d[-1]

        @block.gpsimd
        def _(g):
            for oi in range(NOI):
                g.wait_ge(sV[0], v_d[oi])
                dbo = B2_DB[oi % 2]
                g.dma_start(out=dscr[oi * 128:(oi + 1) * 128, :],
                            in_=arB2[:, dbo:dbo + T]).then_inc(qG[0], 16)
                qG[1] += 16
            assert qG[1] == g_d[-1]

        qS5 = qS[1]
        q_wd = [[qS5 + 16 * ((IT + TT) * hc + i + 1) for i in range(IT)] for hc in range(8)]
        q_xc = [[qS5 + 16 * ((IT + TT) * hc + IT + jt + 1) for jt in range(TT)] for hc in range(8)]
        qA5 = qA[1]
        a_dl = [[qA5 + 16 * ((IT + TT) * hc + i + 1) for i in range(IT)] for hc in range(8)]
        a_ac = [[qA5 + 16 * ((IT + TT) * hc + IT + jt + 1) for jt in range(TT)] for hc in range(8)]
        sP5 = sP[1]
        pe_dn = [[sP5 + IT * hc + i + 1 for i in range(IT)] for hc in range(8)]
        sV7 = sV[1]
        v_ev = [[sV7 + 16 * hc + jt + 1 for jt in range(TT)] for hc in range(8)]
        v_fin = [[sV7 + 16 * hc + 8 + jt + 1 for jt in range(TT)] for hc in range(8)]
        qY5 = qY[1]
        y_st = [[qY5 + 16 * (8 * hc + jt + 1) for jt in range(TT)] for hc in range(8)]

        @block.sync
        def _(q):
            q.wait_ge(qA[0], ph0_all_stored)
            for hc in range(8):
                for i in range(IT):
                    gi = IT * hc + i
                    if gi >= 2:
                        q.wait_ge(sP[0], pe_dn[(gi - 2) // IT][(gi - 2) % IT])
                    wo = B2_WD[i % 2]
                    q.dma_start(out=arB2[:, wo:wo + 512],
                                in_=wdl.ap()[hc * 512:(hc + 1) * 512,
                                             i * 128:(i + 1) * 128],
                                transpose=True).then_inc(qS[0], 16)
                    qS[1] += 16
                for jt in range(TT):
                    gi = 8 * hc + jt
                    if gi >= 2:
                        v_prev = v_fin[(gi - 2) // 8][(gi - 2) % 8]
                        q.wait_ge(sV[0], v_prev)
                    else:
                        q.wait_ge(sV[0], v_acc[-1])
                    xo = AF_XCK[jt % 2]
                    q.dma_start(out=arF[:, xo:xo + 512],
                                in_=x_in[jt * 128:(jt + 1) * 128, hc * 512:(hc + 1) * 512]
                                ).then_inc(qS[0], 16)
                    qS[1] += 16
            assert qS[1] == q_xc[-1][-1]

        @block.scalar
        def _(s):
            s.wait_ge(qG[0], g_d[-1])
            for hc in range(8):
                for i in range(IT):
                    gi = IT * hc + i
                    if gi >= 2:
                        s.wait_ge(sP[0], pe_dn[(gi - 2) // IT][(gi - 2) % IT])
                    do = B2_DLD[i % 2]
                    s.dma_start(out=arB2[:, do:do + T],
                                in_=dscr[i * 128:(i + 1) * 128, :]).then_inc(qA[0], 16)
                    qA[1] += 16
                for jt in range(TT):
                    gi = 8 * hc + jt
                    if gi >= 2:
                        s.wait_ge(sV[0], v_fin[(gi - 2) // 8][(gi - 2) % 8])
                    ao = AF_ACK[jt % 2]
                    s.dma_start(out=arF[:, ao:ao + 512],
                                in_=attn[jt * 128:(jt + 1) * 128, hc * 512:(hc + 1) * 512]
                                ).then_inc(qA[0], 16)
                    qA[1] += 16
            assert qA[1] == a_ac[-1][-1]

        @block.tensor
        def _(t):
            for hc in range(8):
                for i in range(IT):
                    t.wait_ge(qS[0], q_wd[hc][i])
                    t.wait_ge(qA[0], a_dl[hc][i])
                    if hc >= 1 and i == 0:
                        t.wait_ge(sV[0], v_ev[hc - 1][-1])
                    wo = B2_WD[i % 2]
                    do = B2_DLD[i % 2]
                    for jt in range(TT):
                        ps = pg[jt // 4]
                        ins = t.matmul(out=ps[:, (jt % 4) * 512:(jt % 4 + 1) * 512],
                                       lhsT=arB2[:, do + jt * 128: do + (jt + 1) * 128],
                                       rhs=arB2[:, wo:wo + 512],
                                       start=(i == 0), stop=(i == IT - 1))
                    ins.then_inc(sP[0], 1)
                    sP[1] += 1
                assert sP[1] == pe_dn[hc][-1]

        @block.vector
        def _(v):
            for hc in range(8):
                v.wait_ge(sP[0], pe_dn[hc][-1])
                for jt in range(TT):
                    if hc >= 1:
                        v.wait_ge(qY[0], y_st[hc - 1][jt])
                    ps = pg[jt // 4]
                    v.tensor_copy(out=arF[:, AF_DNB + jt * 512: AF_DNB + (jt + 1) * 512],
                                  in_=ps[:, (jt % 4) * 512:(jt % 4 + 1) * 512]
                                  ).then_inc(sV[0], 1)
                    sV[1] += 1
                assert sV[1] == v_ev[hc][-1]
                for jt in range(TT):
                    v.wait_ge(qS[0], q_xc[hc][jt])
                    v.wait_ge(qA[0], a_ac[hc][jt])
                    xo = AF_XCK[jt % 2]
                    ao = AF_ACK[jt % 2]
                    ob = AF_OB[jt % 2]
                    v.drain()
                    v.tensor_add(out=arF[:, ob:ob + 512],
                                 in0=arF[:, AF_DNB + jt * 512: AF_DNB + (jt + 1) * 512],
                                 in1=arF[:, xo:xo + 512])
                    v.drain()
                    v.tensor_add(out=arF[:, AF_DNB + jt * 512: AF_DNB + (jt + 1) * 512],
                                 in0=arF[:, ob:ob + 512],
                                 in1=arF[:, ao:ao + 512])
                    v.drain()
                    v.sem_inc(sV[0], 1)
                    sV[1] += 1
                assert sV[1] == v_fin[hc][-1]

        @block.gpsimd
        def _(g):
            for hc in range(8):
                for jt in range(TT):
                    g.wait_ge(sV[0], v_fin[hc][jt])
                    g.dma_start(out=y_out[jt * 128:(jt + 1) * 128, hc * 512:(hc + 1) * 512],
                                in_=arF[:, AF_DNB + jt * 512: AF_DNB + (jt + 1) * 512]
                                ).then_inc(qY[0], 16)
                    qY[1] += 16
            assert qY[1] == y_st[-1][-1]

        @block.sync
        def _(q):
            q.wait_ge(qY[0], y_st[-1][-1])

    return nc


def kernel(L0_attn_x, L0_ffn_gate_w, L0_ffn_gate_su, L0_ffn_gate_sv,
           L0_ffn_up_w, L0_ffn_up_su, L0_ffn_up_sv,
           L0_ffn_down_w, L0_ffn_down_su, L0_ffn_down_sv):
    x = np.ascontiguousarray(np.asarray(L0_attn_x, dtype=np.float32).reshape(NCORES * T, L))
    gw = np.ascontiguousarray(np.asarray(L0_ffn_gate_w, dtype=np.float32))
    gsu = np.ascontiguousarray(np.asarray(L0_ffn_gate_su, dtype=np.float32))
    gsv = np.ascontiguousarray(np.asarray(L0_ffn_gate_sv, dtype=np.float32))
    uwv = np.ascontiguousarray(np.asarray(L0_ffn_up_w, dtype=np.float32))
    usuv = np.ascontiguousarray(np.asarray(L0_ffn_up_su, dtype=np.float32))
    usvv = np.ascontiguousarray(np.asarray(L0_ffn_up_sv, dtype=np.float32))
    dwv = np.ascontiguousarray(np.asarray(L0_ffn_down_w, dtype=np.float32))
    dsuv = np.ascontiguousarray(np.asarray(L0_ffn_down_su, dtype=np.float32))
    dsvv = np.ascontiguousarray(np.asarray(L0_ffn_down_sv, dtype=np.float32))

    ctab, dtab, ident, rball = _build_tables()

    nc = build_nc()
    in_maps = []
    for c in range(NCORES):
        in_maps.append({
            "x": x[c * T:(c + 1) * T],
            "gw": gw, "gsu": gsu, "gsv": gsv,
            "uw": uwv, "usu": usuv, "usv": usvv,
            "dw": dwv, "dsu": dsuv, "dsv": dsvv,
            "ctab": ctab,
            "dtab": dtab,
            "identd": ident,
            "rbd": rball,
        })
    res = run_bass_kernel_spmd(nc, in_maps, core_ids=list(range(NCORES)))
    y = np.concatenate([res.results[c]["y"] for c in range(NCORES)], axis=0)
    return y.reshape(1, NCORES * T, L)
